# version 03 (final)
# baseline (speedup 1.0000x reference)
import os
import numpy as np
import concourse.bass as bass
import concourse.mybir as mybir
from concourse.bass_utils import run_bass_kernel_spmd

F32 = mybir.dt.float32
BF16 = mybir.dt.bfloat16
ALU = mybir.AluOpType
ACTF = mybir.ActivationFunctionType
AX = mybir.AxisListType.X

B = 2
N = 2048
T = B * N
D = 1024
H, DH, LCH = 16, 64, 8
ITERS = 6
E, ES = 16, 256
NT = T // 128
NTB = N // 128
DEBUG_PAIR = None
KCH = D // 128
QSCALE = float(DH ** -0.5)
ENGS = ("sync", "tensor", "vector", "scalar", "gpsimd")


class Sem:
    def __init__(self, nc, name):
        self.h = nc.alloc_semaphore(name)
        self.n = 0


class KB:
    def __init__(self, ncores=8):
        self.ncores = ncores
        import os
        nc = bass.Bass("TRN2", target_bir_lowering=False,
                       detect_race_conditions=bool(os.environ.get("KRACE")))
        self.nc = nc
        dt = nc.dram_tensor
        self.dx = dt("x", [T, D], F32, kind="ExternalInput").ap()
        self.dwqkv = dt("wqkv", [D, 3 * D], F32, kind="ExternalInput").ap()
        self.dwfc = dt("wfc", [D, D], F32, kind="ExternalInput").ap()
        self.dbfc = dt("bfc", [1, D], F32, kind="ExternalInput").ap()
        self.dselT = dt("selT", [D, E], F32, kind="ExternalInput").ap()
        self.dkeys = dt("keys", [E, D, ES], F32, kind="ExternalInput").ap()
        self.dvals = dt("values", [E, ES, D], F32, kind="ExternalInput").ap()
        self.droute = dt("route", [1, 1], F32, kind="ExternalInput").ap()
        self.dbands = dt("bands", [H, 128, 140], F32, kind="ExternalInput").ap()
        self.dident = dt("identb", [128, 512], F32, kind="ExternalInput").ap()
        self.did128 = dt("id128", [128, 128], F32, kind="ExternalInput").ap()
        self.dpool = dt("pool", [128, 16], F32, kind="ExternalInput").ap()
        self.dout = dt("out", [T, D], F32, kind="ExternalOutput").ap()
        kout = "ExternalOutput" if os.environ.get("KOUT") else "Internal"
        self.ddbg2 = dt("dbg2", [4, 128, 1024], F32, kind=kout).ap()
        self.dxnp = dt("xnp", [KCH, 128, T], BF16, kind=kout).ap()
        self.dapark = dt("apark", [B * H, 128, 512], BF16, kind=kout).ap()
        self.dwpark = dt("wpark", [E, N], BF16, kind=kout).ap()
        self.dmid2 = dt("mid2p", [KCH, 128, N], BF16, kind=kout).ap()
        self.dkeysb = dt("keysb", [E, KCH, 128, ES], BF16).ap()
        self.dvalsb = dt("valsb", [E, 2, 128, D], BF16).ap()
        self.dsval = dt("sval", [1, 2], F32, kind=kout).ap()
        self.dcci = dt("cci", [1, 8], F32).ap()
        self.ddbg = dt("dbgnw", [64, 128, 512], BF16).ap()
        self.ddbg32 = dt("dbgnw32", [40, 128, 512], F32).ap()
        self._dbg_slot = 0
        self._dbg32_slot = 0
        self.dbg_names = []
        self.dcco = dt("cco", [1, 8], F32).ap()

        sb = nc.alloc_sbuf_tensor
        self.identb = sb("identb_s", [128, 512], BF16)
        self.id128 = sb("id128c", [128, 128], BF16)
        self.pool8 = sb("pool8", [128, 16], BF16)
        self.onesr = sb("onesr", [1, 128], BF16)
        self.onesc = sb("onesc", [128, 1], BF16)
        self.bands = sb("bandsc", [128, 2 * 140], BF16)
        self.selT = sb("selTc", [128, KCH * E], BF16)
        self.bfc = sb("bfcc", [1, D], BF16)
        self.route_col = sb("routec", [128, 1], F32)
        self.epsc = sb("epsc", [128, 1], F32)
        self.ql = sb("ql", [128, KCH * 512], BF16)
        self.kl = sb("kl", [128, KCH * 512], BF16)
        self.xn = sb("xn", [128, KCH * N], BF16)
        self.v = sb("vtm", [128, NTB * D], BF16)
        self.scol = sb("scol", [128, 20], F32)
        self.x_in = sb("xin", [128, 2 * D], F32)
        self.ln_st = sb("lnst", [128, 2 * 12], F32)
        self.ln_mv = sb("lnmv", [128, 2 * 12], F32)
        
        self.a2 = sb("a2", [128, 512], BF16)
        self.s2r = sb("s2r", [128, 8], F32)
        self.rowmax = sb("rowmax", [128, 2], F32)
        self.rm_bf = sb("rmbf", [128, 2], BF16)
        self.cmax = sb("cmaxs", [1, 4], F32)
        self.mxs = sb("mxs", [1, 8], F32)
        self.cco_sb = sb("ccosb", [1, 8], F32)
        self.sv = sb("sv", [1, 2], F32)
        self.s_col = sb("svc", [128, 2], F32)
        self.arena = sb("arena", [128, 16384], BF16)
        self.qkw = sb("qkw", [128, 2 * 2 * N], BF16)
        self.e1 = sb("e1", [128, 2 * N], BF16)
        self.e3 = sb("e3", [128, NTB * 256], BF16)
        self.A = sb("Aws", [128, 2 * 512], BF16)
        self.Tn = sb("Tns", [128, 512], BF16)
        self.X = sb("Xns", [128, 2 * 512], BF16)
        self.V = sb("Vns", [128, 512], BF16)
        self.i123 = sb("i123", [128, 2 * 512], BF16)
        self.uT = sb("uTt", [128, 256], BF16)
        self.s3sb = sb("s3sb", [1, 256], BF16)
        self.u = sb("usb", [128, 2 * 64], BF16)
        self.r0 = sb("r0s", [128, 2 * 64], BF16)
        self.r1 = sb("r1s", [128, 2 * 65], BF16)
        self.rec = sb("recq", [128, 8], F32)
        self.midn = sb("midn", [128, NTB * 64], BF16)
        self.probs = sb("probs", [128, NTB * E], F32)
        self.wt1 = sb("wt1", [128, NTB * E], F32)
        self.wt2 = sb("wt2", [128, NTB * E], F32)
        self.wcast = sb("wcast", [128, NTB * E], BF16)
        self.wtm = sb("wtm", [128, NTB], F32)
        self.wneg = sb("wneg", [128, NTB * E], F32)
        self.wbc = sb("wbc", [128, 2 * 2 * 512], BF16)
        self.keyse = sb("keyse", [128, 2 * KCH * 256], BF16)
        self.valse = sb("valse", [128, 2 * 2 * D], BF16)
        self.mid2tb = sb("mid2tb", [128, KCH * 512], BF16)
        self.m2stg = sb("m2stg", [128, 512], BF16)
        self.ostage = sb("ostg", [128, D], F32)
        self.PS = nc.alloc_psum_tensor("P", [128, 4096], F32)
        self.q = self.qkw.ap()[:, 0:2 * N]
        self.k = self.qkw.ap()[:, 2 * N:4 * N]
        self.wfc = self.qkw.ap()
        self.xn_tm = self.e1.ap()[:, 0:2 * D]
        self.xn_stage = self.e1.ap()[:, 2 * D:4 * D]
        self.xl = self.e3.ap()[:, 0:KCH * 512]
        self.wfm = self.e3.ap()[0:16, 0:N]
        self.kvstg = self.arena.ap()[:, 8192:12288].bitcast(F32)
        self.e2 = self.arena.ap()[:, 0:512]
        self.xres = self.x_in.ap()
        self.XT = self.midn.ap()

        self._ksub = int(os.environ.get('KSUB', '99'))
        self._cur_pair = None
        self.ops = {e: [] for e in ENGS}
        self._sems = {}
        self.bg = {i: None for i in range(8)}
        self._fsc_after = {}
        self._fmu_after = {}
        self._fve_after = {}
        self._wbc_rdy = {}
        self._val_rdy = {}
        self._xres_prev = []
        self._ost_prev = []
        self._wpark_prev = 0
        self._m2st_prev = []
        self._pcv_after = {}

    def sem(self, name):
        if name not in self._sems:
            self._sems[name] = Sem(self.nc, name)
        return self._sems[name]

    def emit(self, eng, fn):
        self.ops[eng].append(fn)

    def wait(self, eng, sem, val=None):
        v = sem.n if val is None else val
        if v > 0:
            self.emit(eng, lambda e, s=sem.h, v=v: e.wait_ge(s, v))

    def op(self, eng, build, inc=None, k=1):
        if inc is not None:
            inc.n += k

            def play(e, b=build, s=inc.h, kk=k):
                b(e).then_inc(s, kk)
        else:
            def play(e, b=build):
                b(e)
        self.emit(eng, play)

    def dma(self, eng, out, in_, q):
        q.n += 16
        self.emit(eng, lambda e, o=out, i=in_, s=q.h: e.dma_start(out=o, in_=i)
                  .then_inc(s, 16))

    def dbg_park(self, sbuf_ap, tag=""):
        if self._cur_pair != DEBUG_PAIR:
            return
        s = self._dbg_slot
        self._dbg_slot += 1
        self.dbg_names.append(("b", s, tag))
        p, w = sbuf_ap.shape[0], sbuf_ap.shape[-1]
        self.dma("scalar", self.ddbg[s][0:p, 0:w], sbuf_ap, self.sem("qA"))
        for eng in ("scalar", "vector", "tensor"):
            self.wait(eng, self.sem("qA"))

    def dbg_park32(self, ap, tag="", vwait=None, psum=True):
        if self._cur_pair != DEBUG_PAIR:
            return
        s = self._dbg32_slot
        self._dbg32_slot += 1
        self.dbg_names.append(("f", s, tag))
        p, w = ap.shape[0], ap.shape[-1]
        src = ap
        if psum:
            stg = self.kvstg[:, 0:512][0:p, 0:w]
            dv = self.sem("dbgv")
            if vwait is not None:
                self.wait("vector", vwait)
            self.op("vector", lambda e, o=stg, i=ap: e.tensor_copy(o, i), inc=dv)
            self.wait("scalar", dv)
            src = stg
        self.dma("scalar", self.ddbg32[s][0:p, 0:w], src, self.sem("qA"))
        for eng in ("scalar", "vector", "tensor"):
            self.wait(eng, self.sem("qA"))

    def pb(self, i, w=1):
        return self.PS[:, 512 * i:512 * (i + w)]

    def guard(self, eng, bank):
        g = self.bg[bank]
        if g is not None:
            for sem, cnt in g:
                self.wait(eng, sem, cnt)

    def setg(self, bank, *sems):
        self.bg[bank] = [(s, s.n) for s in sems]

    def build(self):
        stage = int(os.environ.get("KSTAGE", "9"))
        self.s0_consts()
        self.s0_kv_precast()
        if stage >= 2:
            self.s1_ln_transpose()
        if stage >= 3:
            self.s2_landmark_proj()
        if stage >= 4:
            self.s3_attn2_collective()
        khc = int(os.environ.get("KHC", "99"))
        if stage >= 5:
            for b in range(B):
                if stage < 7 and b == 1:
                    break
                self.s4_head(b)
                kpair = int(os.environ.get("KPAIR", "99"))
                for hc in range(KCH):
                    if hc >= khc:
                        break
                    self.s4_qk_chunk(b, hc)
                    if kpair >= 1:
                        self.s5_pair(b, 2 * hc)
                    if kpair >= 2:
                        self.s5_pair(b, 2 * hc + 1)
                if stage >= 6:
                    self.s6_ffn_out(b)
        if stage < 9:
            self.dma("sync", self.dout, self.dx, self.sem("qS"))
        self.finish()
        with self.nc.Block() as block:
            for eng in ENGS:
                lst = self.ops[eng]

                def body(e, lst=lst):
                    for f in lst:
                        f(e)
                getattr(block, eng)(body)
        return self.nc

    def s0_consts(self):
        c = self
        qS, qG = c.sem("qS"), c.sem("qG")
        gcon = c.sem("gcon")
        c.dma("gpsimd", c.identb[:], c.dident, qG)
        c.dma("gpsimd", c.id128[:], c.did128, qG)
        c.dma("gpsimd", c.pool8[:], c.dpool, qG)
        c.dma("gpsimd", c.selT.ap().rearrange("p (k e) -> p k e", k=KCH),
              c.dselT.rearrange("(k p) e -> p k e", k=KCH), qG)
        c.dma("gpsimd", c.bfc[:], c.dbfc, qG)
        c.wait("gpsimd", qG)
        c.op("gpsimd", lambda e: e.memset(c.onesr[:], 1.0))
        c.op("gpsimd", lambda e: e.memset(c.onesc[:], 1.0))
        c.op("gpsimd", lambda e: e.memset(c.r1[:, 64:65], 1.0))
        c.op("gpsimd", lambda e: e.memset(c.r1[:, 129:130], 1.0))
        c.op("gpsimd", lambda e: e.memset(c.epsc[:], 1e-5))
        c.op("gpsimd", lambda e: e.memset(c.rowmax[:], -1e30), inc=gcon)
        c.dma("sync", c.route_col[:, 0:1], c.droute.broadcast_to([128, 1]), qS)
        c.op("vector", lambda e: e.memset(c.wneg[:], -1.0))
        c.op("vector", lambda e: e.memset(c.cmax[:], -1e30))

    def s0_kv_precast(self):
        c = self
        qS, qA = c.sem("qS"), c.sem("qA")
        kv = c.sem("kvc")
        jobs = []
        for e in range(E):
            for kc in range(KCH):
                jobs.append((c.dkeys[e, 128 * kc:128 * (kc + 1), 0:256],
                             c.dkeysb[e, kc, :, 0:256], 256))
            for ch in range(2):
                for o in (0, 512):
                    jobs.append((c.dvals[e, 128 * ch:128 * (ch + 1), o:o + 512],
                                 c.dvalsb[e, ch, :, o:o + 512], 512))
        base_l, base_s = qS.n, qA.n
        for j, (src, dst, w) in enumerate(jobs):
            if j >= 4:
                c.wait("sync", kv, j - 3)
            c.dma("sync", c.kvstg[:, 512 * (j % 4):512 * (j % 4) + w], src, qS)
            c.wait("scalar", qS, base_l + 16 * (j + 1))
            if j >= 8:
                c.wait("scalar", qA, base_s + 16 * (j - 7))
            c.op("scalar",
                 lambda e, j=j, w=w: e.activation(
                     out=c.arena[:, 512 * (j % 8):512 * (j % 8) + w],
                     in_=c.kvstg[:, 512 * (j % 4):512 * (j % 4) + w],
                     func=ACTF.Copy), inc=kv)
            c.wait("scalar", kv, j + 1)
            c.dma("scalar", dst, c.arena[:, 512 * (j % 8):512 * (j % 8) + w], qA)
        self.kv_store_end = qA.n

    def s1_ln_transpose(self):
        c = self
        qS = c.sem("qS")
        lnv, lns, lnr, lnx, lnq = (c.sem(k) for k in
                                   ("lnv", "lns", "lnr", "lnx", "lnq"))
        ttp, tev, xlev = (c.sem(k) for k in ("ttp", "tev", "xlev"))
        x_base = qS.n
        store_done = {}
        load_done = {}
        for i in range(NT):
            sl = i % 2
            xv = c.x_in[:, D * sl:D * (sl + 1)]
            st = c.ln_st[:, 12 * sl:12 * (sl + 1)]
            mv = c.ln_mv[:, 12 * sl:12 * (sl + 1)]
            if i >= 2:
                c.wait("sync", lnx, i - 1)
            c.dma("sync", xv, c.dx[128 * i:128 * (i + 1), :], qS)
            load_done[i] = qS.n
            scr32 = c.arena[:, 4096 + 2048 * sl:4096 + 2048 * (sl + 1)].bitcast(F32)
            c.wait("vector", qS, load_done[i])
            def vgap():
                for _ in range(4):
                    c.op("vector", lambda e: e.memset(c.rec[:, 3:4], 0.0))
            c.op("vector", lambda e, scr32=scr32, xv=xv: e.tensor_tensor(
                out=scr32, in0=xv, in1=xv, op=ALU.mult))
            c.op("vector", lambda e, xv=xv, mv=mv: e.reduce_sum(
                mv[:, 6:7], xv, axis=AX))
            vgap()
            c.op("vector", lambda e, scr32=scr32, mv=mv: e.reduce_sum(
                mv[:, 5:6], scr32, axis=AX))
            vgap()
            if os.environ.get("KOUT") and i < 3:
                c.op("vector", lambda e: e.memset(c.rec[:, 3:4], 0.0), inc=lnq)
                c.wait("scalar", lnq)
                c.dma("scalar", c.ddbg2[i], xv, c.sem("qA"))
            c.op("vector", lambda e, mv=mv: e.tensor_scalar(
                out=mv[:, 0:1], in0=mv[:, 6:7], scalar1=1.0 / D, scalar2=None,
                op0=ALU.mult))
            c.op("vector", lambda e, mv=mv: e.tensor_scalar(
                out=mv[:, 7:8], in0=mv[:, 5:6], scalar1=1.0 / D, scalar2=None,
                op0=ALU.mult))
            vgap()
            c.op("vector", lambda e, mv=mv: e.tensor_scalar(
                out=mv[:, 8:9], in0=mv[:, 0:1], scalar1=mv[:, 0:1], scalar2=None,
                op0=ALU.mult))
            vgap()
            c.op("vector", lambda e, mv=mv: e.tensor_tensor(
                out=mv[:, 1:2], in0=mv[:, 7:8], in1=mv[:, 8:9],
                op=ALU.subtract), inc=lnv)
            c.wait("scalar", lnv)
            if i == 0:
                c.wait("scalar", c.sem("gcon"), 1)
            c.op("scalar", lambda e, mv=mv: e.activation(out=mv[:, 2:3], in_=mv[:, 1:2],
                                                         func=ACTF.Ln,
                                                         bias=c.epsc[:, 0:1]))
            for _ in range(4):
                c.op("scalar", lambda e: e.activation(
                    out=c.rec[:, 4:5], in_=c.epsc[:, 0:1], func=ACTF.Copy))
            c.op("scalar", lambda e, mv=mv: e.activation(out=mv[:, 3:4], in_=mv[:, 2:3],
                                                         func=ACTF.Exp, scale=-0.5),
                 inc=lns)
            c.wait("vector", lns)
            c.op("vector", lambda e, mv=mv: e.scalar_tensor_tensor(
                out=mv[:, 4:5], in0=mv[:, 0:1], scalar=-1.0, in1=mv[:, 3:4],
                op0=ALU.mult, op1=ALU.mult), inc=lnr)
            c.wait("scalar", lnr)
            if os.environ.get("KOUT") and i < 3:
                c.dma("scalar", c.ddbg2[3][:, 32 * i:32 * i + 12], mv,
                      c.sem("qA"))
            if i >= 2:
                c.wait("scalar", ttp, i - 1)
            c.op("scalar", lambda e, mv=mv, xv=xv, sl=sl: e.activation(
                out=c.xn_tm[:, D * sl:D * (sl + 1)], in_=xv, func=ACTF.Identity,
                scale=mv[:, 3:4], bias=mv[:, 4:5]), inc=lnx)
            if i == 0:
                c.wait("tensor", c.sem("gcon"), 1)
            c.wait("tensor", lnx)
            if i >= 2:
                c.wait("tensor", tev, i - 1)
            if i % 8 == 0 and i >= 8:
                c.wait("tensor", xlev, i // 8)
            for fc in range(KCH):
                lhs = c.xn_tm[:, D * sl + 128 * fc:D * sl + 128 * (fc + 1)]
                bank = 2 * sl + fc // 4
                c.op("tensor", lambda e, lhs=lhs, bank=bank, fc=fc: e.matmul(
                    c.pb(bank)[:, 128 * (fc % 4):128 * (fc % 4) + 128],
                    lhs, c.id128[:], start=True, stop=True))
                c.op("tensor", lambda e, lhs=lhs, fc=fc, ti=i % 8: e.matmul(
                    c.pb(4 + fc // 4)[:, 128 * (fc % 4) + 16 * ti:
                                      128 * (fc % 4) + 16 * (ti + 1)],
                    lhs, c.pool8[:], start=True, stop=True),
                     inc=ttp if fc == KCH - 1 else None)
            c.wait("vector", ttp)
            if i >= 2:
                c.wait("vector", qS, store_done[i - 2])
            c.op("vector", lambda e, sl=sl: e.tensor_copy(
                c.xn_stage[:, 1024 * sl:1024 * sl + 512], c.pb(2 * sl)))
            c.op("vector", lambda e, sl=sl: e.tensor_copy(
                c.xn_stage[:, 1024 * sl + 512:1024 * (sl + 1)], c.pb(2 * sl + 1)),
                 inc=tev)
            c.wait("sync", tev)
            c.dma("sync",
                  c.dxnp.rearrange("f p t -> p f t")[:, :, 128 * i:128 * (i + 1)],
                  c.xn_stage.rearrange("p (s f w) -> p s f w", s=2, f=KCH)
                  [:, sl, :, :], qS)
            store_done[i] = qS.n
            if i % 8 == 7:
                g_ = i // 8
                c.wait("vector", ttp)
                c.op("vector", lambda e, g_=g_: e.tensor_copy(
                    c.xl.rearrange("p (f l) -> p f l", f=KCH)
                    [:, 0:4, 128 * g_:128 * (g_ + 1)],
                    c.pb(4).rearrange("p (f w) -> p f w", f=4)))
                c.op("vector", lambda e, g_=g_: e.tensor_copy(
                    c.xl.rearrange("p (f l) -> p f l", f=KCH)
                    [:, 4:8, 128 * g_:128 * (g_ + 1)],
                    c.pb(5).rearrange("p (f w) -> p f w", f=4)), inc=xlev)
        for i in range(8):
            self.bg[i] = None
        self.setg(0, xlev)
        for i in range(8):
            self.bg[i] = [(tev, NT)] if i < 4 else [(xlev, 4)]

    def s2_landmark_proj(self):
        c = self
        qG = c.sem("qG")
        s2mm, s2ev = c.sem("s2mm"), c.sem("s2ev")
        part_base = qG.n
        c.wait("gpsimd", c.sem("kvc"))
        c.wait("gpsimd", c.sem("qA"), self.kv_store_end)
        c.wait("gpsimd", c.sem("lnx"), NT)
        for part in range(2):
            for kc in range(KCH):
                c.dma("gpsimd",
                      c.arena[:, 8192 * part + 1024 * kc:8192 * part + 1024 * (kc + 1)],
                      c.dwqkv[128 * kc:128 * (kc + 1), 1024 * part:1024 * (part + 1)],
                      qG)
        c.wait("tensor", c.sem("xlev"), 4)
        for part in range(2):
            dst = c.ql if part == 0 else c.kl
            scale = QSCALE if part == 0 else 1.0
            for m in range(KCH):
                idx = part * 8 + m
                bank = 6 + idx % 2
                c.wait("tensor", qG, part_base + 16 * KCH * (part + 1))
                c.guard("tensor", bank)
                for kc in range(KCH):
                    lhs = c.arena[:, 8192 * part + 1024 * kc + 128 * m:
                                  8192 * part + 1024 * kc + 128 * (m + 1)]
                    c.op("tensor", lambda e, lhs=lhs, bank=bank, kc=kc: e.matmul(
                        c.pb(bank), lhs, c.xl[:, 512 * kc:512 * (kc + 1)],
                        start=(kc == 0), stop=(kc == KCH - 1)),
                         inc=s2mm if kc == KCH - 1 else None)
                c.wait("scalar", s2mm)
                c.op("scalar", lambda e, dst=dst, m=m, bank=bank, scale=scale:
                     e.activation(out=dst[:, 512 * m:512 * (m + 1)], in_=c.pb(bank),
                                  func=ACTF.Copy, scale=scale), inc=s2ev)
                c.setg(bank, s2ev)

    def s3_attn2_collective(self):
        c = self
        qS, qG, qA = c.sem("qS"), c.sem("qG"), c.sem("qA")
        a2mm, a2ex, a2rc, a2no = (c.sem(k) for k in ("a2mm", "a2ex", "a2rc", "a2no"))
        a2cs, a2mx = c.sem("a2cs"), c.sem("a2mx")
        self.apark_base = qA.n
        for p in range(B * H):
            b, h = p // H, p % H
            hc, hb = h // 2, 64 * (h % 2)
            bank = p % 2
            if p == 0:
                c.wait("tensor", c.sem("s2ev"), 16)
            c.guard("tensor", bank)
            for mi in range(2):
                c.op("tensor", lambda e, hb=hb, hc=hc, b=b, mi=mi, bank=bank:
                     e.matmul(c.pb(bank)[:, 256 * mi:256 * (mi + 1)],
                              c.ql[hb:hb + 64, 512 * hc + 256 * b + 128 * mi:
                                   512 * hc + 256 * b + 128 * (mi + 1)],
                              c.kl[hb:hb + 64, 512 * hc + 256 * b:
                                   512 * hc + 256 * (b + 1)],
                              start=True, stop=True),
                     inc=a2mm if mi == 1 else None)
            c.wait("scalar", a2mm)
            if p >= 1:
                c.wait("scalar", qA, self.apark_base + 16 * p)
            for mi in range(2):
                c.op("scalar", lambda e, mi=mi, bank=bank: e.activation(
                    out=c.e2[:, 256 * mi:256 * (mi + 1)],
                    in_=c.pb(bank)[:, 256 * mi:256 * (mi + 1)],
                    func=ACTF.Exp, accum_out=c.s2r[:, mi:mi + 1]),
                     inc=a2ex if mi == 1 else None)
            c.setg(bank, a2ex)
            c.wait("vector", a2ex)
            c.op("vector", lambda e: e.reciprocal(c.s2r[:, 2:4], c.s2r[:, 0:2]),
                 inc=a2rc)
            c.wait("scalar", a2rc)
            if p >= 1:
                c.wait("scalar", a2mx, p)
            for mi in range(2):
                c.op("scalar", lambda e, mi=mi: e.activation(
                    out=c.a2[:, 256 * mi:256 * (mi + 1)],
                    in_=c.e2[:, 256 * mi:256 * (mi + 1)],
                    func=ACTF.Identity, scale=c.s2r[:, 2 + mi:3 + mi],
                    accum_out=c.s2r[:, 4 + mi:5 + mi]),
                     inc=a2no if mi == 1 else None)
            c.wait("tensor", a2no)
            if p >= 1:
                c.wait("tensor", a2mx, p)
            for mi in range(2):
                c.op("tensor", lambda e, mi=mi: e.matmul(
                    c.pb(2)[0:1, 0:256], c.onesc[:],
                    c.a2[:, 256 * mi:256 * (mi + 1)],
                    start=(mi == 0), stop=(mi == 1)),
                     inc=a2cs if mi == 1 else None)
            c.wait("vector", a2no)
            c.op("vector", lambda e: e.tensor_tensor(
                out=c.rowmax[:], in0=c.rowmax[:], in1=c.s2r[:, 4:6], op=ALU.max))
            c.wait("vector", a2cs)
            c.op("vector", lambda e: e.reduce_max(c.cmax[:, 1:2],
                                                  c.pb(2)[0:1, 0:256], axis=AX))
            for _ in range(4):
                c.op("vector", lambda e: e.memset(c.rec[:, 3:4], 0.0))
            c.op("vector", lambda e: e.tensor_tensor(
                out=c.cmax[:, 0:1], in0=c.cmax[:, 0:1], in1=c.cmax[:, 1:2],
                op=ALU.max), inc=a2mx)
            c.wait("scalar", a2no)
            c.dma("scalar", c.dapark[p], c.a2[:], qA)
        rmf, rmt, mxr = c.sem("rmf"), c.sem("rmt"), c.sem("mxr")
        c.wait("vector", a2mx, B * H)
        c.op("vector", lambda e: e.reduce_max(c.rowmax[:, 0:1], c.rowmax[:, 0:2],
                                              axis=AX))
        for _ in range(4):
            c.op("vector", lambda e: e.memset(c.rec[:, 3:4], 0.0))
        c.op("vector", lambda e: e.tensor_copy(c.rm_bf[:, 0:2], c.rowmax[:, 0:2]),
             inc=rmf)
        c.wait("tensor", rmf)
        c.guard("tensor", 3)
        c.op("tensor", lambda e: e.matmul(c.pb(3)[0:1, 0:128], c.rm_bf[:, 0:1],
                                          c.id128[:], start=True, stop=True),
             inc=rmt)
        c.wait("vector", rmt)
        c.op("vector", lambda e: e.reduce_max(c.mxs[:, 0:1], c.pb(3)[0:1, 0:128],
                                              axis=AX))
        c.op("vector", lambda e: e.tensor_copy(c.mxs[:, 1:2], c.cmax[:, 0:1]))
        c.op("vector", lambda e: e.memset(c.mxs[:, 2:8], 0.0), inc=mxr)
        ccd, svr, scl = c.sem("ccd"), c.sem("svr"), c.sem("scl")
        c.wait("gpsimd", mxr)
        c.dma("gpsimd", c.dcci, c.mxs[:], qG)
        c.wait("gpsimd", qG)
        c.op("gpsimd", lambda e: e.collective_compute(
            "AllReduce", ALU.max, replica_groups=[list(range(self.ncores))],
            ins=[c.dcci], outs=[c.dcco]), inc=ccd)
        c.wait("gpsimd", ccd)
        c.dma("gpsimd", c.cco_sb[:], c.dcco, qG)
        c.wait("vector", qG)
        c.op("vector", lambda e: e.tensor_tensor(
            out=c.sv[:, 0:1], in0=c.cco_sb[:, 0:1], in1=c.cco_sb[:, 1:2],
            op=ALU.mult))
        for _ in range(4):
            c.op("vector", lambda e: e.memset(c.rec[:, 3:4], 0.0))
        c.op("vector", lambda e: e.tensor_copy(c.sv[:, 1:2], c.sv[:, 0:1]))
        for _ in range(4):
            c.op("vector", lambda e: e.memset(c.rec[:, 3:4], 0.0))
        c.op("vector", lambda e: e.reciprocal(c.sv[:, 0:2], c.sv[:, 0:2]), inc=svr)
        c.wait("scalar", svr)
        c.dma("scalar", c.dsval, c.sv[:], c.sem("qA"))
        c.wait("sync", c.sem("qA"))
        c.dma("sync", c.s_col[:, 0:1],
              c.dsval[0:1, 0:1].broadcast_to([128, 1]), qS)
        c.wait("vector", qS)
        for kk in range(ITERS):
            cm = 3.25 ** kk
            for j, f in enumerate((-cm / 7.0, -7.0 * cm / 15.0, -15.0 * cm / 13.0)):
                c.op("vector", lambda e, kk=kk, j=j, f=f: e.tensor_scalar(
                    out=c.scol[:, 3 * kk + j:3 * kk + j + 1], in0=c.s_col[:, 0:1],
                    scalar1=float(f), scalar2=None, op0=ALU.mult))
        c.op("vector", lambda e: e.tensor_scalar(
            out=c.scol[:, 18:19], in0=c.s_col[:, 0:1],
            scalar1=float(3.25 ** ITERS), scalar2=None, op0=ALU.mult), inc=scl)
        self.bg[0] = [(a2ex, a2ex.n)]
        self.bg[1] = [(a2ex, a2ex.n)]
        self.bg[2] = [(a2mx, a2mx.n)]
        self.bg[3] = [(mxr, 1)]

    def s4_head(self, b):
        c = self
        qS, qG = c.sem("qS"), c.sem("qG")
        vmm, vev, selsg, wtop = (c.sem(k) for k in ("vmm", "vev", "selsg", "wtop"))
        foe, fve = c.sem("foe"), c.sem("fve")
        if b == 1:
            c.wait("sync", foe)
        for fc in range(KCH):
            c.dma("sync", c.xn[:, N * fc:N * (fc + 1)],
                  c.dxnp[fc, :, N * b:N * (b + 1)], qS)
        xn_rdy = qS.n
        if b == 0:
            c.wait("gpsimd", c.sem("s2ev"), 16)
        else:
            c.wait("gpsimd", fve)
        for kc in range(KCH):
            c.dma("gpsimd", c.arena[:, 1024 * kc:1024 * (kc + 1)],
                  c.dwqkv[128 * kc:128 * (kc + 1), 2048:3072], qG)
        base_vev = vev.n
        c.wait("tensor", qG)
        c.wait("tensor", qS, xn_rdy)
        if b == 1:
            c.wait("tensor", foe)
        for i in range(NTB):
            if i >= 2:
                c.wait("tensor", vev, base_vev + i - 1)
            for bk in range(2):
                c.guard("tensor", 2 * (i % 2) + bk)
            if i < 2:
                c.guard("tensor", 4)
            for kc in range(KCH):
                lhs = c.xn[:, N * kc + 128 * i:N * kc + 128 * (i + 1)]
                for half in range(2):
                    c.op("tensor", lambda e, lhs=lhs, i=i, half=half, kc=kc:
                         e.matmul(c.pb(2 * (i % 2) + half), lhs,
                                  c.arena[:, 1024 * kc + 512 * half:
                                          1024 * kc + 512 * (half + 1)],
                                  start=(kc == 0), stop=(kc == KCH - 1)))
                c.op("tensor", lambda e, lhs=lhs, i=i, kc=kc: e.matmul(
                    c.pb(4)[:, 16 * i:16 * (i + 1)], lhs,
                    c.selT[:, E * kc:E * (kc + 1)],
                    start=(kc == 0), stop=(kc == KCH - 1)),
                     inc=vmm if kc == KCH - 1 else None)
            c.wait("vector", vmm)
            c.op("vector", lambda e, i=i: e.tensor_copy(
                c.v[:, D * i:D * (i + 1)], c.pb(2 * (i % 2), 2)), inc=vev)
            c.setg(2 * (i % 2), vev)
            c.setg(2 * (i % 2) + 1, vev)
        c.wait("scalar", vmm)
        c.op("scalar", lambda e: e.activation(out=c.probs[:],
                                              in_=c.pb(4)[:, 0:NTB * E],
                                              func=ACTF.Sigmoid), inc=selsg)
        self.setg(4, selsg)
        pr = c.probs.ap().rearrange("p (t e) -> p t e", e=E)
        cur = c.wt1.ap().rearrange("p (t e) -> p t e", e=E)
        msk = c.wt2.ap().rearrange("p (t e) -> p t e", e=E)
        neg = c.wneg.ap().rearrange("p (t e) -> p t e", e=E)
        mxb = c.wtm.ap().unsqueeze(2).broadcast_to([128, NTB, E])
        def vgap():
            for _ in range(4):
                c.op("vector", lambda e: e.memset(c.rec[:, 3:4], 0.0))
        c.wait("vector", selsg)
        c.op("vector", lambda e: e.tensor_copy(c.wt1[:], c.probs[:]))
        vgap()
        for r in range(4):
            c.op("vector", lambda e: e.reduce_max(c.wtm[:], cur, axis=AX))
            vgap()
            if r < 3:
                c.op("vector", lambda e: e.tensor_tensor(out=msk, in0=cur, in1=mxb,
                                                         op=ALU.is_ge))
                vgap()
                c.op("vector", lambda e: e.scalar_tensor_tensor(
                    out=c.wt1[:], in0=c.wt2[:], scalar=c.wneg[:, 0:1],
                    in1=c.wt1[:], op0=ALU.mult, op1=ALU.add))
                vgap()
        c.op("vector", lambda e: e.tensor_tensor(out=msk, in0=pr, in1=mxb,
                                                 op=ALU.is_ge))
        vgap()
        c.op("vector", lambda e: e.tensor_tensor(out=c.wt1[:], in0=c.probs[:],
                                                 in1=c.wt2[:], op=ALU.mult))
        vgap()
        c.op("vector", lambda e: e.reduce_sum(c.wtm[:], cur, axis=AX))
        vgap()
        c.op("vector", lambda e: e.tensor_scalar(out=c.wtm[:], in0=c.wtm[:],
                                                 scalar1=1e-9, scalar2=None,
                                                 op0=ALU.max))
        vgap()
        c.op("vector", lambda e: e.reciprocal(c.wtm[:], c.wtm[:]))
        vgap()
        c.op("vector", lambda e: e.tensor_scalar(out=c.wtm[:], in0=c.wtm[:],
                                                 scalar1=c.route_col[:, 0:1],
                                                 scalar2=None, op0=ALU.mult))
        vgap()
        c.op("vector", lambda e: e.tensor_tensor(out=cur, in0=cur, in1=mxb,
                                                 op=ALU.mult))
        vgap()
        c.op("vector", lambda e: e.tensor_copy(c.wcast[:], c.wt1[:]), inc=wtop)
        self.qk_base = qG.n
        for part in range(2):
            half = 1 - part
            if part == 1:
                c.wait("gpsimd", vmm)
            for kc in range(KCH):
                c.dma("gpsimd",
                      c.arena[:, 8192 * half + 1024 * kc:8192 * half + 1024 * (kc + 1)],
                      c.dwqkv[128 * kc:128 * (kc + 1), 1024 * part:1024 * (part + 1)],
                      qG)

    def s4_qk_chunk(self, b, hc):
        c = self
        qkmm, qkev = c.sem("qkmm"), c.sem("qkev")
        for part in range(2):
            half = 1 - part
            dst = c.q if part == 0 else c.k
            c.wait("tensor", c.sem("qG"), self.qk_base + 16 * KCH * (part + 1))
            banks = [4 * ((2 * hc + part) % 2) + nb for nb in range(4)]
            for bk in banks:
                c.guard("tensor", bk)
            for kc in range(KCH):
                lhs = c.arena[:, 8192 * half + 1024 * kc + 128 * hc:
                              8192 * half + 1024 * kc + 128 * (hc + 1)]
                for nb in range(4):
                    c.op("tensor", lambda e, lhs=lhs, bk=banks[nb], kc=kc, nb=nb:
                         e.matmul(c.pb(bk), lhs,
                                  c.xn[:, N * kc + 512 * nb:N * kc + 512 * (nb + 1)],
                                  start=(kc == 0), stop=(kc == KCH - 1)),
                         inc=qkmm if (kc == KCH - 1 and nb == 3) else None)
            c.wait("scalar", qkmm)
            c.op("scalar", lambda e, dst=dst, hc=hc, b0=banks[0], part=part:
                 e.activation(out=dst[:, N * (hc % 2):N * (hc % 2 + 1)],
                              in_=c.pb(b0, 4), func=ACTF.Copy,
                              scale=QSCALE if part == 0 else 1.0), inc=qkev)
            for bk in banks:
                c.setg(bk, qkev)
        self.qkev_rdy = qkev.n

    def s5_pair(self, b, h):
        c = self
        self._cur_pair = (b, h)
        hc, hb = h // 2, 64 * (h % 2)
        par = hc % 2
        p = b * H + h
        qv = c.q[:, N * par:N * (par + 1)]
        kv = c.k[:, N * par:N * (par + 1)]
        qlsl = c.ql[hb:hb + 64, 512 * hc + 256 * b:512 * hc + 256 * (b + 1)]
        asl = c.A[:, 512 * (p % 2):512 * (p % 2 + 1)]
        S = c.sem
        qS, qA = S("qS"), S("qA")
        p_s3, p_e3 = S("p_s3"), S("p_e3")
        p_s1, p_e1 = S("p_s1"), S("p_e1")
        p_T, p_Te, p_uT, p_ue, p_ut, p_rc, p_un = (
            S("p_T"), S("p_Te"), S("p_uT"), S("p_ue"), S("p_ut"), S("p_rc"), S("p_un"))
        p_r0, p_re = S("p_r0"), S("p_re")
        p_nw, p_nv, p_ni = S("p_nw"), S("p_nv"), S("p_ni")
        p_r1, p_r1e = S("p_r1"), S("p_r1e")
        p_gp, p_gn, p_cv, p_m2 = S("p_gp"), S("p_gn"), S("p_cv"), S("p_m2")

        prevp = self._pcv_after.get(p - 2)
        if prevp is not None:
            c.wait("gpsimd", p_cv, prevp)
        c.dma("gpsimd", c.bands[:, 140 * (h % 2):140 * (h % 2) + 140],
              c.dbands[h], S("qG"))
        bands_rdy = S("qG").n
        c.wait("sync", qA, self.apark_base + 16 * (p + 1))
        if p >= 2:
            c.wait("sync", p_r0, p - 1)
        c.dma("sync", asl, c.dapark[p], qS)
        a_rdy = qS.n

        c.wait("tensor", S("qkev"), self.qkev_rdy)
        for grp in range(8):
            bank = 4 + grp % 2
            c.guard("tensor", bank)
            for j in range(2):
                ti = 2 * grp + j
                c.op("tensor", lambda e, ti=ti, j=j, bank=bank, kv=kv, qlsl=qlsl:
                     e.matmul(c.pb(bank)[:, 256 * j:256 * (j + 1)],
                              kv[hb:hb + 64, 128 * ti:128 * (ti + 1)], qlsl,
                              start=True, stop=True),
                     inc=p_s3 if j == 1 else None)
            c.wait("scalar", p_s3)
            c.op("scalar", lambda e, grp=grp, bank=bank: e.activation(
                out=c.e3[:, 512 * grp:512 * (grp + 1)], in_=c.pb(bank),
                func=ACTF.Exp), inc=p_e3)
            c.setg(bank, p_e3)
        if self._ksub < 1:
            return
        ke1 = os.environ.get("KE1", "")
        for q_ in range(int(os.environ.get("KQ1", "8"))):
            mi, nb = q_ // 4, q_ % 4
            bank = (4 if "bk4" in ke1 else 6) + q_ % 2
            c.guard("tensor", bank)
            lhs1 = (c.id128[0:64, 0:128] if "lhsid" in ke1 else
                    c.kl[hb:hb + 64, 512 * hc + 256 * b + 128 * mi:
                         512 * hc + 256 * b + 128 * (mi + 1)])
            rhs1 = (c.xn[0:64, 512 * nb:512 * (nb + 1)] if "rhsxn" in ke1 else
                    qv[hb:hb + 64, 512 * nb:512 * (nb + 1)])
            if "nomm" not in ke1:
                c.op("tensor", lambda e, bank=bank, lhs1=lhs1, rhs1=rhs1:
                     e.matmul(c.pb(bank), lhs1, rhs1,
                              start=True, stop=True), inc=p_s1)
                c.wait("scalar", p_s1)
            if "noact" not in ke1:
                c.op("scalar", lambda e, mi=mi, nb=nb, bank=bank: e.activation(
                    out=c.e1[:, N * mi + 512 * nb:N * mi + 512 * (nb + 1)],
                    in_=c.pb(bank), func=ACTF.Exp), inc=p_e1)
                c.setg(bank, p_e1)
        if self._ksub < 2:
            return
        kt = os.environ.get("KT", "")
        c.wait("tensor", qS, a_rdy)
        c.guard("tensor", 4)
        if "cp" in kt:
            c.wait("vector", qS, a_rdy)
            c.op("vector", lambda e, asl=asl: e.tensor_copy(
                c.i123[:, 512:1024], asl), inc=p_ni)
            c.wait("tensor", p_ni)
        for mi in range(2):
            for kc in range(2):
                rhsT = (c.i123[:, 512:1024] if "cp" in kt else asl)
                c.op("tensor", lambda e, mi=mi, kc=kc, asl=asl, rhsT=rhsT: e.matmul(
                    c.pb(4)[:, 256 * mi:256 * (mi + 1)],
                    asl[:, 256 * kc + 128 * mi:256 * kc + 128 * (mi + 1)],
                    rhsT[:, 256 * kc:256 * (kc + 1)],
                    start=(kc == 0), stop=(kc == 1)),
                     inc=p_T if (mi == 1 and kc == 1) else None)
        c.wait("scalar", p_T)
        c.dbg_park32(c.pb(4), "T", vwait=p_T)
        if "noevac" not in kt:
            c.op("scalar", lambda e: e.activation(out=c.Tn[:], in_=c.pb(4),
                                                  func=ACTF.Copy), inc=p_Te)
        c.wait("vector", p_T)
        c.wait("vector", p_Te)
        if "noscl" not in kt:
            c.wait("vector", S("scl"), 1)
        if "nostt" not in kt:
            c.op("vector", lambda e: e.scalar_tensor_tensor(
                out=c.i123[:, 0:512], in0=c.pb(4), scalar=c.scol[:, 0:1],
                in1=c.identb[:], op0=ALU.mult, op1=ALU.add), inc=p_ni)
        self.setg(4, p_ni, p_Te)
        if self._ksub < 3:
            return
        c.wait("tensor", p_e3)
        c.guard("tensor", 5)
        c.guard("tensor", 6)
        for tc in range(NTB):
            c.op("tensor", lambda e, tc=tc, h=h: e.matmul(
                c.pb(5)[0:64, 0:256],
                c.v[:, D * tc + 64 * h:D * tc + 64 * (h + 1)],
                c.e3[:, 256 * tc:256 * (tc + 1)],
                start=(tc == 0), stop=(tc == NTB - 1)))
            c.op("tensor", lambda e, tc=tc: e.matmul(
                c.pb(6)[0:1, 0:256], c.onesc[:],
                c.e3[:, 256 * tc:256 * (tc + 1)],
                start=(tc == 0), stop=(tc == NTB - 1)),
                 inc=p_uT if tc == NTB - 1 else None)
        c.wait("vector", p_uT)
        c.op("vector", lambda e: e.tensor_copy(c.uT[0:64, :], c.pb(5)[0:64, 0:256]))
        c.op("vector", lambda e: e.tensor_copy(c.s3sb[:], c.pb(6)[0:1, 0:256]),
             inc=p_ue)
        self.setg(5, p_ue)
        self.setg(6, p_ue)
        if self._ksub < 4:
            return
        c.wait("tensor", p_Te)
        c.wait("tensor", p_ue)
        c.guard("tensor", 4)
        c.guard("tensor", 6)
        for mi in range(2):
            c.op("tensor", lambda e, mi=mi: e.matmul(
                c.pb(4)[:, 256 * mi:256 * mi + 64],
                c.uT[0:64, 128 * mi:128 * (mi + 1)],
                c.id128[0:64, 0:64], start=True, stop=True))
            c.op("tensor", lambda e, mi=mi: e.matmul(
                c.pb(6)[:, 256 * mi:256 * mi + 1],
                c.s3sb[0:1, 128 * mi:128 * (mi + 1)],
                c.onesr[0:1, 0:1], start=True, stop=True),
                 inc=p_ut if mi == 1 else None)
        c.wait("vector", p_ut)
        c.op("vector", lambda e: e.reciprocal(
            c.rec[:, 0:2],
            c.pb(6).rearrange("p (a w) -> p a w", a=2)[:, :, 0:1]
            .squeeze(2)), inc=p_rc)
        self.setg(6, p_rc)
        c.wait("scalar", p_rc)
        for mi in range(2):
            c.op("scalar", lambda e, mi=mi: e.activation(
                out=c.u[:, 64 * mi:64 * (mi + 1)],
                in_=c.pb(4)[:, 256 * mi:256 * mi + 64],
                func=ACTF.Identity, scale=c.rec[:, mi:mi + 1]),
                 inc=p_un if mi == 1 else None)
        self.setg(4, p_un, p_rc)
        if self._ksub < 5:
            return
        c.wait("tensor", p_un)
        c.guard("tensor", 5)
        for mj in range(2):
            for kc in range(2):
                c.op("tensor", lambda e, mj=mj, kc=kc, asl=asl: e.matmul(
                    c.pb(5)[:, 256 * mj:256 * mj + 64],
                    asl[:, 256 * kc + 128 * mj:256 * kc + 128 * (mj + 1)],
                    c.u[:, 64 * kc:64 * (kc + 1)],
                    start=(kc == 0), stop=(kc == 1)),
                     inc=p_r0 if (mj == 1 and kc == 1) else None)
        c.wait("vector", p_r0)
        c.op("vector", lambda e: e.tensor_copy(
            c.r0.ap().rearrange("p (a w) -> p a w", a=2),
            c.pb(5).rearrange("p (a w) -> p a w", a=2)[:, :, 0:64]), inc=p_re)
        self.setg(5, p_re)
        if self._ksub < 6:
            return
        nbk = [4, 5]
        bi = [0]

        def product(lhs_t, rhs_t, inc_sem):
            bank = nbk[bi[0] % 2]
            bi[0] += 1
            c.guard("tensor", bank)
            for mi in range(2):
                for kc in range(2):
                    c.op("tensor",
                         lambda e, mi=mi, kc=kc, lhs_t=lhs_t, rhs_t=rhs_t, bank=bank:
                         e.matmul(c.pb(bank)[:, 256 * mi:256 * (mi + 1)],
                                  lhs_t[:, 256 * kc + 128 * mi:256 * kc + 128 * (mi + 1)],
                                  rhs_t[:, 256 * kc:256 * (kc + 1)],
                                  start=(kc == 0), stop=(kc == 1)),
                         inc=inc_sem if (mi == 1 and kc == 1) else None)
            return bank

        i1 = c.i123[:, 0:512]
        i2 = c.i123[:, 512:1024]
        for it in range(ITERS):
            Xt = c.identb if it == 0 else c.X[:, 512 * (it % 2):512 * (it % 2) + 512]
            XTt = c.identb if it == 0 else c.XT[:, 512 * (it % 2):512 * (it % 2) + 512]
            i3dst = c.X[:, 512 * ((it + 1) % 2):512 * ((it + 1) % 2) + 512]
            xtdst = c.XT[:, 512 * ((it + 1) % 2):512 * ((it + 1) % 2) + 512]
            if it >= 1:
                c.wait("tensor", p_nv)
                bkv = product(c.Tn, Xt, p_nw)
                bkw = product(Xt, c.Tn, p_nw)
                c.wait("scalar", p_nw)
                c.dbg_park32(c.pb(bkv), "pV", vwait=p_nw)
                c.dbg_park32(c.pb(bkw), "pW", vwait=p_nw)
                c.op("scalar", lambda e, bkw=bkw: e.activation(
                    out=c.V[:], in_=c.pb(bkw), func=ACTF.Copy), inc=p_nv)
                c.setg(bkw, p_nv)
                c.wait("scalar", p_nv)
                c.dbg_park(c.V[:], "Wevac")
                c.wait("vector", p_nw)
                c.op("vector", lambda e, bkv=bkv, it=it: e.scalar_tensor_tensor(
                    out=i1, in0=c.pb(bkv), scalar=c.scol[:, 3 * it:3 * it + 1],
                    in1=c.identb[:], op0=ALU.mult, op1=ALU.add), inc=p_ni)
                c.setg(bkv, p_ni)
                c.wait("scalar", p_ni)
                c.dbg_park(i1, "i1")
            Wt = c.Tn if it == 0 else c.V
            c.wait("tensor", p_ni)
            if it >= 1:
                c.wait("tensor", p_nv)
            c.dbg_park(Wt[:, 0:512], "Wpre")
            c.dbg_park(i1, "i1pre")
            bk = product(Wt, i1, p_nw)
            c.wait("scalar", p_nw)
            c.dbg_park32(c.pb(bk), "p2", vwait=p_nw)
            c.wait("vector", p_nw)
            c.op("vector", lambda e, bk=bk, it=it: e.scalar_tensor_tensor(
                out=i2, in0=c.pb(bk), scalar=c.scol[:, 3 * it + 1:3 * it + 2],
                in1=c.identb[:], op0=ALU.mult, op1=ALU.add), inc=p_ni)
            c.setg(bk, p_ni)
            c.wait("scalar", p_ni)
            c.dbg_park(i2, "i2")
            c.wait("tensor", p_ni)
            bk = product(Wt, i2, p_nw)
            c.wait("scalar", p_nw)
            c.dbg_park32(c.pb(bk), "p3", vwait=p_nw)
            c.wait("vector", p_nw)
            c.op("vector", lambda e, bk=bk, it=it, i3dst=i3dst:
                 e.scalar_tensor_tensor(
                     out=i3dst if it == 0 else i1, in0=c.pb(bk),
                     scalar=c.scol[:, 3 * it + 2:3 * it + 3],
                     in1=c.identb[:], op0=ALU.mult, op1=ALU.add), inc=p_ni)
            c.setg(bk, p_ni)
            c.wait("scalar", p_ni)
            c.dbg_park(i3dst if it == 0 else i1, "i3")
            if it >= 1:
                i3sl = i1
                c.wait("tensor", p_ni)
                bk4 = product(XTt, i3sl, p_nw)
                bk5 = product(i3sl, XTt, p_nw)
                c.wait("scalar", p_nw)
                c.dbg_park32(c.pb(bk4), "p4", vwait=p_nw)
                c.dbg_park32(c.pb(bk5), "pXT", vwait=p_nw)
                c.op("scalar", lambda e, bk4=bk4, i3dst=i3dst: e.activation(
                    out=i3dst, in_=c.pb(bk4), func=ACTF.Copy), inc=p_nv)
                c.op("scalar", lambda e, bk5=bk5, xtdst=xtdst: e.activation(
                    out=xtdst, in_=c.pb(bk5), func=ACTF.Copy), inc=p_nv)
                c.setg(bk4, p_nv)
                c.setg(bk5, p_nv)
                c.wait("scalar", p_nv)
                c.dbg_park(i3dst, "X")
                c.dbg_park(xtdst, "XT")
            else:
                c.wait("tensor", p_ni)
                bk = product(i2, c.Tn, p_nw)
                c.wait("scalar", p_nw)
                c.dbg_park32(c.pb(bk), "pXT", vwait=p_nw)
                c.wait("vector", p_nw)
                c.op("vector", lambda e, bk=bk, xtdst=xtdst:
                     e.scalar_tensor_tensor(
                         out=xtdst, in0=c.pb(bk),
                         scalar=c.scol[:, 2:3],
                         in1=c.identb[:], op0=ALU.mult, op1=ALU.add), inc=p_ni)
                c.setg(bk, p_ni)
                c.wait("scalar", p_ni)
                c.dbg_park(xtdst, "XT")
                c.op("vector", lambda e: e.memset(c.rec[:, 2:3], 0.0),
                     inc=p_nv, k=2)
        if self._ksub < 7:
            return
        X6 = c.XT[:, 512 * (ITERS % 2):512 * (ITERS % 2) + 512]
        if DEBUG_PAIR is not None and self._cur_pair == DEBUG_PAIR:
            c.wait("scalar", p_re)
            c.dbg_park(c.r0[:], "r0")
            c.dbg_park(c.u[:], "u")
            c.dbg_park32(c.scol[:, 0:19], "scol", psum=False)
        c.wait("tensor", p_nv)
        c.wait("tensor", p_re)
        bank = nbk[bi[0] % 2]
        bi[0] += 1
        c.guard("tensor", bank)
        for mj in range(2):
            for kc in range(2):
                c.op("tensor", lambda e, mj=mj, kc=kc, X6=X6, bank=bank: e.matmul(
                    c.pb(bank)[:, 256 * mj:256 * mj + 64],
                    X6[:, 256 * kc + 128 * mj:256 * kc + 128 * (mj + 1)],
                    c.r0[:, 64 * kc:64 * (kc + 1)],
                    start=(kc == 0), stop=(kc == 1)),
                     inc=p_r1 if (mj == 1 and kc == 1) else None)
        c.wait("scalar", p_r1)
        c.dbg_park32(c.pb(bank)[:, 0:64], "pr1a", vwait=p_r1)
        c.dbg_park32(c.pb(bank)[:, 256:320], "pr1b", vwait=p_r1)
        c.op("scalar", lambda e, bank=bank: e.activation(
            out=c.r1.ap().rearrange("p (a w) -> p a w", a=2)[:, :, 0:64],
            in_=c.pb(bank).rearrange("p (a w) -> p a w", a=2)[:, :, 0:64],
            func=ACTF.Identity, scale=c.scol[:, 18:19]), inc=p_r1e)
        self.setg(bank, p_r1e)
        if DEBUG_PAIR is not None and self._cur_pair == DEBUG_PAIR:
            c.wait("scalar", p_r1e)
            c.dbg_park(c.r1[:], "r1")
        if self._ksub < 8:
            return
        c.wait("tensor", p_r1e)
        c.wait("tensor", p_e1)
        for grp in range(4):
            bank = nbk[bi[0] % 2]
            bi[0] += 1
            c.guard("tensor", bank)
            for j in range(4):
                tt = 4 * grp + j
                for mi in range(2):
                    c.op("tensor", lambda e, tt=tt, j=j, mi=mi, bank=bank: e.matmul(
                        c.pb(bank)[:, 128 * j:128 * j + 65],
                        c.e1[:, N * mi + 128 * tt:N * mi + 128 * (tt + 1)],
                        c.r1[:, 65 * mi:65 * (mi + 1)],
                        start=(mi == 0), stop=(mi == 1)),
                         inc=p_gp if (j == 3 and mi == 1) else None)
            c.wait("vector", p_gp)
            c.op("vector", lambda e, bank=bank: e.reciprocal(
                c.rec[:, 4:8],
                c.pb(bank).rearrange("p (a w) -> p a w", a=4)[:, :, 64:65].squeeze(2)))
            for _ in range(4):
                c.op("vector", lambda e: e.memset(c.rec[:, 3:4], 0.0))
            c.op("vector", lambda e, bank=bank, grp=grp: e.tensor_tensor(
                out=c.midn.ap().rearrange("p (t w) -> p t w", w=64)
                [:, 4 * grp:4 * (grp + 1), :],
                in0=c.pb(bank).rearrange("p (a w) -> p a w", a=4)[:, :, 0:64],
                in1=c.rec.ap()[:, 4:8].unsqueeze(2).broadcast_to([128, 4, 64]),
                op=ALU.mult), inc=p_gn)
            c.setg(bank, p_gn)
        c.wait("tensor", p_gn)
        c.wait("tensor", S("qG"), bands_rdy)
        bsl = c.bands[:, 140 * (h % 2):140 * (h % 2) + 140]
        tp = None if h % 2 == 0 else (0, 64)
        prt = 64 * (h % 2)
        for grp in range(4):
            bank = 6 + grp % 2
            c.guard("tensor", bank)
            for j in range(4):
                tt = 4 * grp + j
                tile_ops = []
                tile_ops.append(("mid", tt))
                if tt > 0:
                    tile_ops.append(("prev", tt))
                if tt < NTB - 1:
                    tile_ops.append(("next", tt))
                for oi, (kind, tt_) in enumerate(tile_ops):
                    first = (oi == 0)
                    lastop = (oi == len(tile_ops) - 1)
                    inc = p_cv if (j == 3 and lastop) else None
                    if kind == "mid":
                        c.op("tensor", lambda e, tt_=tt_, j=j, bank=bank, bsl=bsl,
                             tp=tp, prt=prt, first=first, lastop=lastop: e.matmul(
                                 c.pb(bank)[prt:prt + 64, 128 * j:128 * (j + 1)],
                                 c.midn[:, 64 * tt_:64 * (tt_ + 1)], bsl[:, 0:128],
                                 start=first, stop=lastop, tile_position=tp), inc=inc)
                    elif kind == "prev":
                        c.op("tensor", lambda e, tt_=tt_, j=j, bank=bank, bsl=bsl,
                             tp=tp, prt=prt, first=first, lastop=lastop: e.matmul(
                                 c.pb(bank)[prt:prt + 64, 128 * j:128 * j + 6],
                                 c.midn[:, 64 * (tt_ - 1):64 * tt_], bsl[:, 128:134],
                                 start=first, stop=lastop, tile_position=tp), inc=inc)
                    else:
                        c.op("tensor", lambda e, tt_=tt_, j=j, bank=bank, bsl=bsl,
                             tp=tp, prt=prt, first=first, lastop=lastop: e.matmul(
                                 c.pb(bank)[prt:prt + 64, 128 * j + 122:128 * (j + 1)],
                                 c.midn[:, 64 * (tt_ + 1):64 * (tt_ + 2)],
                                 bsl[:, 134:140],
                                 start=first, stop=lastop, tile_position=tp), inc=inc)
            c.wait("vector", p_cv)
            if self._m2st_prev:
                c.wait("vector", S("qA"), self._m2st_prev[-1])
            c.op("vector", lambda e, bank=bank, grp=grp, prt=prt:
                 e.tensor_copy(
                     c.m2stg[prt:prt + 64, 0:512],
                     c.pb(bank)[prt:prt + 64, :]), inc=p_m2)
            c.setg(bank, p_m2)
            c.wait("scalar", p_m2)
            c.dma("scalar", c.dmid2[hc, prt:prt + 64, 512 * grp:512 * (grp + 1)],
                  c.m2stg[prt:prt + 64, 0:512],
                  S("qA"))
            self._m2st_prev.append(S("qA").n)
        self._pcv_after[p] = p_cv.n

    def s6_ffn_out(self, b):
        c = self
        S = c.sem
        qS, qA = S("qS"), S("qA")
        wtrm, wfme = S("wtrm"), S("wfme")
        fsc, fre, fmu, fve, foe = (S(k) for k in ("fsc", "fre", "fmu", "fve", "foe"))
        fkl, fvl, fwl, fxl = (S(k) for k in ("fkl", "fvl", "fwl", "fxl"))
        c.wait("tensor", S("wtop"))
        for g4 in range(4):
            bank = g4 % 2
            c.guard("tensor", bank)
            for j in range(4):
                i = 4 * g4 + j
                c.op("tensor", lambda e, i=i, j=j, bank=bank: e.matmul(
                    c.pb(bank)[0:16, 128 * j:128 * (j + 1)],
                    c.wcast[:, E * i:E * (i + 1)], c.id128[:],
                    start=True, stop=True), inc=wtrm if j == 3 else None)
            c.wait("vector", wtrm)
            if b == 1 and g4 == 0:
                c.wait("vector", qA, self._wpark_prev)
            c.op("vector", lambda e, g4=g4, bank=bank: e.tensor_copy(
                c.wfm[0:16, 512 * g4:512 * (g4 + 1)], c.pb(bank)[0:16, :]),
                 inc=wfme)
            c.setg(bank, wfme)
        if b == 1:
            pass
        c.wait("scalar", wfme)
        c.dma("scalar", c.dwpark[:, :], c.wfm[0:16, :], qA)
        wpark_rdy = qA.n
        self._wpark_prev = wpark_rdy
        m2park_all = qA.n
        qG = S("qG")
        c.wait("gpsimd", S("p_s1"))
        for kc in range(KCH):
            c.dma("gpsimd", c.wfc[:, 1024 * kc:1024 * (kc + 1)],
                  c.dwfc[128 * kc:128 * (kc + 1), :], qG)
        wfc_rdy = qG.n

        key_rdy = {}

        def load_keys(tb, e_):
            prev = ((b, tb, e_ - 2) if e_ >= 2 else
                    (b, tb - 1, E - 2 + e_) if tb > 0 else
                    (b - 1, 3, E - 2 + e_))
            g = self._fsc_after.get(prev)
            assert g is not None or (b == 0 and tb == 0 and e_ < 2), prev
            if g is not None:
                c.wait("sync", fsc, g)
            c.dma("sync",
                  c.keyse.ap().rearrange("p (s k w) -> p s k w", s=2, k=KCH)
                  [:, e_ % 2, :, :],
                  c.dkeysb[e_].rearrange("k p w -> p k w"), qS)
            key_rdy[e_] = qS.n

        def load_wbc(tb, eg):
            prev = ((b, tb, eg - 2) if eg >= 2 else
                    (b, tb - 1, 6 + eg) if tb > 0 else
                    (b - 1, 3, 6 + eg))
            gm = self._fmu_after.get(prev)
            assert gm is not None or (b == 0 and tb == 0 and eg < 2), prev
            if gm is not None:
                c.wait("sync", fmu, gm)
            c.wait("sync", qA, wpark_rdy)
            c.dma("sync",
                  c.wbc.ap().rearrange("p (s g w) -> p s g w", s=2, g=2)
                  [:, eg % 2, :, :],
                  c.dwpark[2 * eg:2 * (eg + 1),
                           512 * tb:512 * (tb + 1)].unsqueeze(0)
                  .broadcast_to([128, 2, 512]), qS)
            self._wbc_rdy[(b, tb, eg)] = qS.n

        def load_vals(tb, e_):
            prev = ((b, tb, e_ - 2) if e_ >= 2 else
                    (b, tb - 1, E - 2 + e_) if tb > 0 else
                    (b - 1, 3, E - 2 + e_))
            gv = self._fve_after.get(prev)
            assert gv is not None or (b == 0 and tb == 0 and e_ < 2), prev
            if gv is not None:
                c.wait("sync", fve, gv)
            c.dma("sync",
                  c.valse.ap().rearrange("p (s ch d) -> p s ch d", s=2, ch=2)
                  [:, e_ % 2, :, :],
                  c.dvalsb[e_].rearrange("ch p d -> p ch d"), qS)
            self._val_rdy[(b, tb, e_)] = qS.n

        for tb in range(4):
            for e_ in range(2):
                load_keys(tb, e_)
            for eg in range(2):
                load_wbc(tb, eg)
            for e_ in range(E):
                for es in range(2):
                    sidx = 2 * e_ + es
                    bank = sidx % 8
                    c.wait("tensor", qS, key_rdy[e_])
                    c.guard("tensor", bank)
                    for kc in range(KCH):
                        c.op("tensor", lambda e, e_=e_, es=es, kc=kc, bank=bank,
                             tb=tb: e.matmul(
                                 c.pb(bank),
                                 c.keyse[:, 2048 * (e_ % 2) + 256 * kc + 128 * es:
                                         2048 * (e_ % 2) + 256 * kc + 128 * (es + 1)],
                                 c.xn[:, N * kc + 512 * tb:N * kc + 512 * (tb + 1)],
                                 start=(kc == 0), stop=(kc == KCH - 1)),
                             inc=fsc if kc == KCH - 1 else None)
                    self._fsc_after[(b, tb, e_)] = fsc.n
                    c.wait("scalar", fsc)
                    c.op("scalar", lambda e, sidx=sidx, bank=bank: e.activation(
                        out=c.arena[:, 512 * sidx:512 * (sidx + 1)],
                        in_=c.pb(bank), func=ACTF.Relu), inc=fre)
                    c.setg(bank, fre)
                    c.wait("vector", fre)
                    c.wait("vector", qS, self._wbc_rdy[(b, tb, e_ // 2)])
                    c.op("vector", lambda e, sidx=sidx, e_=e_: e.tensor_tensor(
                        out=c.arena[:, 512 * sidx:512 * (sidx + 1)],
                        in0=c.arena[:, 512 * sidx:512 * (sidx + 1)],
                        in1=c.wbc[:, 1024 * ((e_ // 2) % 2) + 512 * (e_ % 2):
                                  1024 * ((e_ // 2) % 2) + 512 * (e_ % 2) + 512],
                        op=ALU.mult), inc=fmu)
                self._fmu_after[(b, tb, e_ // 2)] = fmu.n
                if e_ + 2 < E:
                    load_keys(tb, e_ + 2)
                if e_ % 2 == 1 and e_ // 2 + 2 < 8:
                    load_wbc(tb, e_ // 2 + 2)
            for e_ in range(2):
                load_vals(tb, e_)
            c.wait("sync", qA, m2park_all)
            if tb > 0 or b == 1:
                c.wait("sync", fve)
            for kc in range(KCH):
                c.dma("sync", c.mid2tb[:, 512 * kc:512 * (kc + 1)],
                      c.dmid2[kc, :, 512 * tb:512 * (tb + 1)], qS)
            m2tb_rdy = qS.n
            for tt in range(4):
                for half in range(2):
                    c.guard("tensor", 2 * tt + half)
            c.wait("tensor", qS, m2tb_rdy)
            c.wait("tensor", qG, wfc_rdy)
            for kc in range(KCH):
                for tt in range(4):
                    lhs = c.mid2tb[:, 512 * kc + 128 * tt:512 * kc + 128 * (tt + 1)]
                    for half in range(2):
                        c.op("tensor", lambda e, lhs=lhs, tt=tt, half=half, kc=kc:
                             e.matmul(c.pb(2 * tt + half), lhs,
                                      c.wfc[:, D * kc + 512 * half:
                                            D * kc + 512 * (half + 1)],
                                      start=(kc == 0), stop=False))
            for tt in range(4):
                for half in range(2):
                    c.op("tensor", lambda e, tt=tt, half=half: e.matmul(
                        c.pb(2 * tt + half), c.onesr[0:1, 0:128],
                        c.bfc[0:1, 512 * half:512 * (half + 1)],
                        start=False, stop=False))
            for e_ in range(E):
                c.wait("tensor", qS, self._val_rdy[(b, tb, e_)])
                if e_ == 0:
                    c.wait("tensor", fmu)
                last_e = (e_ == E - 1)
                for es in range(2):
                    sidx = 2 * e_ + es
                    for tt in range(4):
                        for half in range(2):
                            last = last_e and es == 1
                            c.op("tensor", lambda e, sidx=sidx, tt=tt, half=half,
                                 es=es, e_=e_, last=last: e.matmul(
                                     c.pb(2 * tt + half),
                                     c.arena[:, 512 * sidx + 128 * tt:
                                             512 * sidx + 128 * (tt + 1)],
                                     c.valse[:, 2048 * (e_ % 2) + 1024 * es +
                                             512 * half:2048 * (e_ % 2) +
                                             1024 * es + 512 * (half + 1)],
                                     start=False, stop=last),
                                 inc=fve if (es == 1 and tt == 3 and half == 1)
                                 else None)
                self._fve_after[(b, tb, e_)] = fve.n
                if e_ + 2 < E:
                    load_vals(tb, e_ + 2)
            for tt in range(4):
                gidx = NTB * b + 4 * tb + tt
                xsl = c.xres[:, D * (tt % 2):D * (tt % 2 + 1)]
                if len(self._xres_prev) >= 2:
                    c.wait("sync", foe, self._xres_prev[-2])
                c.dma("sync", xsl, c.dx[128 * gidx:128 * (gidx + 1), :], qS)
                xr_rdy = qS.n
                c.wait("vector", fve)
                c.wait("vector", qS, xr_rdy)
                if self._ost_prev:
                    c.wait("vector", qA, self._ost_prev[-1])
                c.op("vector", lambda e, tt=tt, xsl=xsl: e.tensor_add(
                    c.ostage[:], c.pb(2 * tt, 2), xsl), inc=foe)
                self._xres_prev.append(foe.n)
                c.wait("scalar", foe)
                c.dma("scalar", c.dout[128 * gidx:128 * (gidx + 1), :],
                      c.ostage[:], qA)
                self._ost_prev.append(qA.n)
                c.setg(2 * tt, foe)
                c.setg(2 * tt + 1, foe)

    def finish(self):
        c = self
        for eng, q in (("scalar", "qA"), ("gpsimd", "qG"), ("sync", "qS")):
            c.wait(eng, c.sem(q))


def host_constants():
    ident = np.zeros((128, 512), np.float32)
    ident[np.arange(128), np.arange(128)] = 1.0
    ident[np.arange(128), 384 + np.arange(128)] = 1.0
    id128 = np.eye(128, dtype=np.float32)
    pool = np.zeros((128, 16), np.float32)
    pool[np.arange(128), np.arange(128) // 8] = 1.0 / LCH
    return ident, id128, pool


def host_bands(w_conv):
    bands = np.zeros((H, 128, 140), np.float32)
    for hh in range(H):
        w = w_conv[hh, 0, :, 0]
        for r in range(128):
            for cc in range(max(0, r - 6), min(128, r + 7)):
                bands[hh, r, cc] = w[r - cc + 6]
            bands[hh, r, r] += 1.0
            for cc in range(6):
                d = (r - 128) - cc
                if -6 <= d <= 6:
                    bands[hh, r, 128 + cc] = w[d + 6]
            for cl in range(6):
                d = (r + 128) - (122 + cl)
                if -6 <= d <= 6:
                    bands[hh, r, 134 + cl] = w[d + 6]
    return bands


def kernel(L0_attn_x, L0_attn_w_qkv, L0_attn_w_conv, L0_attn_w_fc, L0_attn_b_fc,
           L0_ffn_expert_sel, L0_ffn_keys, L0_ffn_values, L0_ffn_route_scale):
    ident, id128, pool = host_constants()
    bands = host_bands(np.asarray(L0_attn_w_conv))
    x = np.ascontiguousarray(np.asarray(L0_attn_x), dtype=np.float32)
    common = {
        "wqkv": np.ascontiguousarray(L0_attn_w_qkv, dtype=np.float32),
        "wfc": np.ascontiguousarray(L0_attn_w_fc, dtype=np.float32),
        "bfc": np.ascontiguousarray(L0_attn_b_fc, dtype=np.float32).reshape(1, D),
        "selT": np.ascontiguousarray(np.asarray(L0_ffn_expert_sel).T, dtype=np.float32),
        "keys": np.ascontiguousarray(L0_ffn_keys, dtype=np.float32),
        "values": np.ascontiguousarray(L0_ffn_values, dtype=np.float32),
        "route": np.ascontiguousarray(L0_ffn_route_scale, dtype=np.float32).reshape(1, 1),
        "bands": bands, "identb": ident, "id128": id128, "pool": pool,
    }
    kb = KB()
    nc = kb.build()
    in_maps = []
    for cid in range(8):
        m = dict(common)
        m["x"] = x[2 * cid:2 * cid + 2].reshape(T, D)
        in_maps.append(m)
    res = run_bass_kernel_spmd(nc, in_maps, core_ids=list(range(8)))
    return np.concatenate([r["out"].reshape(2, N, D) for r in res.results], axis=0)
